# version 25
# baseline (speedup 1.0000x reference)
import math
import contextlib
import numpy as np
import ml_dtypes
import concourse.bass as bass
import concourse.mybir as mybir
from concourse.bass_utils import run_bass_kernel_spmd

F32 = mybir.dt.float32
BF16 = mybir.dt.bfloat16
ALU = mybir.AluOpType
AF = mybir.ActivationFunctionType
AX = mybir.AxisListType

C_RND = float(np.float32(1.5 * 2 ** 23))
EPS = 1e-5
N_CORES = 8


class Sem:
    def __init__(self, es, nc, name):
        self.h = es.enter_context(nc.semaphore(name))
        self.n = 0
        self.name = name

    def bump(self, k=1):
        self.n += k
        return self.n


class Prog:
    ENG = ("sync", "scalar", "vector", "tensor", "gpsimd")

    def __init__(self):
        self.q = {e: [] for e in self.ENG}
        self.wm = {e: {} for e in self.ENG}

    def wait(self, eng, sem, val):
        if val <= 0:
            return
        if self.wm[eng].get(sem.name, -1) >= val:
            return
        self.wm[eng][sem.name] = val
        self.q[eng].append(lambda e, s=sem.h, v=val: e.wait_ge(s, v))

    def op(self, eng, build, inc=None, inck=1, waits=()):
        for (s, v) in waits:
            self.wait(eng, s, v)
        if inc is not None:
            t = inc.bump(inck)
            self.q[eng].append(lambda e, b=build, s=inc.h, k=inck: b(e).then_inc(s, k))
            return t
        self.q[eng].append(lambda e, b=build: b(e))
        return None

    def dma(self, eng, dst, src, sem, waits=()):
        return self.op(eng, lambda e, d=dst, s=src: e.dma_start(d, s),
                       inc=sem, inck=16, waits=waits)


def build_program(T=2048, D=4096, I=11008, NH=32, DQK=128, NP=4):
    DK = D // 128
    NS = I // 128
    NT = T // 128
    PT = T // NP
    TSP = PT // 128
    NQ = 8
    QD = DQK // NQ
    QW = 3 * DQK
    NW = float(I * D)
    DQCH = min(D, max(128, min(1024, (8 * 512) // TSP)))
    NDQ = D // DQCH
    MMN = min(512, DQCH)
    NCH = DQCH // MMN
    TRG = DK // 4
    s_att = float(np.float32(1.0) / np.float32(math.sqrt(float(DQK))))
    QF = min(2048, D)
    QCH = D // QF
    PQF = min(2048, D)
    PQ = D // PQF

    nc = bass.Bass("TRN2", target_bir_lowering=False)

    x0_d = nc.dram_tensor("x0", [T, D], F32, kind="ExternalInput").ap()
    x0t_d = nc.dram_tensor("x0t", [T // 128, 128, D], F32, kind="ExternalInput").ap()
    wq_d = nc.dram_tensor("wq", [D, DQK * NH], F32, kind="ExternalInput").ap()
    bq_d = nc.dram_tensor("bq", [1, DQK * NH], F32, kind="ExternalInput").ap()
    wk_d = nc.dram_tensor("wk", [D, DQK], F32, kind="ExternalInput").ap()
    bk_d = nc.dram_tensor("bk", [1, DQK], F32, kind="ExternalInput").ap()
    wv_d = nc.dram_tensor("wv", [D, DQK], F32, kind="ExternalInput").ap()
    bv_d = nc.dram_tensor("bv", [1, DQK], F32, kind="ExternalInput").ap()
    wg_d = nc.dram_tensor("wg", [NS, 128, D], F32, kind="ExternalInput").ap()
    wu_d = nc.dram_tensor("wu", [NS, 128, D], F32, kind="ExternalInput").ap()
    wd_d = nc.dram_tensor("wd", [NS, 128, D], F32, kind="ExternalInput").ap()
    id_d = nc.dram_tensor("ident", [128, 128], F32, kind="ExternalInput").ap()
    out_d = nc.dram_tensor("out", [T, D], F32, kind="ExternalOutput").ap()
    x1b_d = nc.dram_tensor("x1b_hbm", [T, D], BF16, kind="Internal").ap()
    qc_d = [nc.dram_tensor(n, [NS, 128, D], BF16, kind="Internal").ap()
            for n in ("qg_c", "qu_c", "qd_c")]

    es = contextlib.ExitStack()

    sq_s = [Sem(es, nc, f"sq{i}") for i in range(4)]
    sp_s = [Sem(es, nc, f"sp{i}") for i in range(4)]
    qw_s = [Sem(es, nc, f"qw{i}") for i in range(3)]
    qc_s = [Sem(es, nc, f"qc{i}") for i in range(3)]
    xa_s = [Sem(es, nc, f"xa{i}") for i in range(2)]
    xw_s = [Sem(es, nc, f"xw{i}") for i in range(3)]
    xm_s = [Sem(es, nc, f"xm{i}") for i in range(2)]
    s_tg = Sem(es, nc, "s_tg")
    s_r2 = Sem(es, nc, "s_r2")
    s_out = Sem(es, nc, "s_out")
    s_mi = Sem(es, nc, "s_mi")
    vd = Sem(es, nc, "vd")
    va = Sem(es, nc, "va")
    vp = Sem(es, nc, "vp")
    vg = Sem(es, nc, "vg")

    P = Prog()

    rmbc = es.enter_context(nc.sbuf_tensor("rmbc", [128, 4], F32))
    cbias = es.enter_context(nc.sbuf_tensor("cbias", [128, 4], F32))

    esA = contextlib.ExitStack()
    ident_sb = esA.enter_context(nc.sbuf_tensor("id_sb", [128, 128], F32))
    ones_row = esA.enter_context(nc.sbuf_tensor("ones_row", [1, 128], BF16))
    onesf_row = esA.enter_context(nc.sbuf_tensor("onesf_row", [1, 128], F32))
    onesf_col = esA.enter_context(nc.sbuf_tensor("onesf_col", [128, 1], F32))
    brow = esA.enter_context(nc.sbuf_tensor("brow", [1, QW], BF16))
    srow = esA.enter_context(nc.sbuf_tensor("srow", [1, 16], F32))
    BQC = min(1024, DQK * NH)
    btmp = esA.enter_context(nc.sbuf_tensor("btmp", [1, BQC], F32))
    btmp2 = esA.enter_context(nc.sbuf_tensor("btmp2", [1, DQK], F32))

    P.op("vector", lambda e: e.memset(cbias[:, 0:1], 1.0), inc=vd)
    P.op("vector", lambda e: e.memset(cbias[:, 1:2], 2.0), inc=vd)
    P.op("vector", lambda e: e.memset(cbias[:, 2:3], C_RND - 1.0), inc=vd)
    P.op("vector", lambda e: e.memset(cbias[:, 3:4], -C_RND), inc=vd)
    P.op("vector", lambda e: e.memset(ones_row[:], 1.0), inc=vd)
    P.op("vector", lambda e: e.memset(onesf_row[:], 1.0), inc=vd)
    P.op("vector", lambda e: e.memset(onesf_col[:], 1.0), inc=vd)
    t_id = P.dma("scalar", ident_sb[:], id_d[:], s_mi)

    for c in range((DQK * NH) // BQC):
        dpc = BQC // NH
        t_bq = P.dma("scalar", btmp[:], bq_d[0:1, c * BQC:(c + 1) * BQC], s_mi,
                     waits=[(s_mi, s_mi.n)])
        P.op("vector",
             lambda e, c=c, dpc=dpc: e.reduce_sum(
                 btmp2[0:1, c * dpc:(c + 1) * dpc],
                 btmp[:].rearrange("p (d h) -> p d h", h=NH), axis=AX.X),
             inc=vd, waits=[(s_mi, t_bq), (vd, vd.n)])
    P.op("vector", lambda e: e.tensor_scalar(brow[0:1, 0:DQK], btmp2[:], s_att, None, ALU.mult),
         inc=vd, waits=[(vd, vd.n)])
    t_bk = P.dma("scalar", btmp2[:], bk_d[:], s_mi, waits=[(vd, vd.n)])
    P.op("vector", lambda e: e.tensor_copy(brow[0:1, DQK:2 * DQK], btmp2[:]),
         inc=vd, waits=[(s_mi, t_bk)])
    t_bv = P.dma("scalar", btmp2[:], bv_d[:], s_mi, waits=[(vd, vd.n)])
    P.op("vector", lambda e: e.tensor_copy(brow[0:1, 2 * DQK:3 * DQK], btmp2[:]),
         inc=vd, waits=[(s_mi, t_bv)])

    wcomb = esA.enter_context(nc.sbuf_tensor("wcomb", [128, DK * QW], BF16))
    parts = esA.enter_context(nc.sbuf_tensor("parts", [128, 3 * NS * PQ + 6], F32))
    pr_big = [esA.enter_context(nc.sbuf_tensor(f"prb{i}", [128, PQF], F32)) for i in range(4)]
    pjunk = esA.enter_context(nc.sbuf_tensor("pjunk", [128, PQF], BF16))
    pjunk2 = esA.enter_context(nc.sbuf_tensor("pjunk2", [128, PQF], BF16))
    x0_sb = [esA.enter_context(nc.sbuf_tensor(f"x0_{i}", [128, D], F32)) for i in range(2)]
    xt_sb = [esA.enter_context(nc.sbuf_tensor(f"xt_{i}", [128, D], BF16)) for i in range(2)]
    r1_sb = [esA.enter_context(nc.sbuf_tensor(f"r1_{i}", [128, D], F32)) for i in range(2)]
    S_sb = [esA.enter_context(nc.sbuf_tensor(f"S_{i}", [128, QD * DQK], F32)) for i in range(2)]
    A_sb = esA.enter_context(nc.sbuf_tensor("A_sb", [128, max(QD * DQK, D)], BF16))
    A2_sb = esA.enter_context(nc.sbuf_tensor("A2_sb", [128, QD * DQK], BF16))
    qkvt = [esA.enter_context(nc.sbuf_tensor(f"qkvt{i}", [128, QW], F32)) for i in range(2)]
    vbf = esA.enter_context(nc.sbuf_tensor("vbf", [128, DQK], BF16))
    numb = esA.enter_context(nc.sbuf_tensor("numb", [128, DQK], F32))
    denb = esA.enter_context(nc.sbuf_tensor("denb", [128, DQK], F32))
    sM = esA.enter_context(nc.sbuf_tensor("sM", [128, 4], F32))
    wqs_f = esA.enter_context(nc.sbuf_tensor("wqs_f", [128, max(PQF // NH, 1)], F32))
    ps_tr = [esA.enter_context(nc.psum_tensor(f"ps_tr{i}", [128, 512], F32)) for i in range(2)]
    ps_qkv = [esA.enter_context(nc.psum_tensor(f"ps_qkv{i}", [128, QW], F32)) for i in range(2)]
    ps_sc = esA.enter_context(nc.psum_tensor("ps_sc", [1, 16], F32))
    ps_bc = esA.enter_context(nc.psum_tensor("ps_bc", [128, 4], F32))

    prb_free = [(vd, 0)] * 4
    qraw_free = [(vd, 0)] * 4

    for k in range(DK):
        for c in range(PQ):
            sl = (k * PQ + c) % 4
            t_ld = P.dma("scalar", pr_big[sl][:],
                         wq_d[k * 128:(k + 1) * 128, c * PQF:(c + 1) * PQF], sp_s[sl],
                         waits=[prb_free[sl]])
            P.op("vector",
                 lambda e, sl=sl: e.reduce_sum(
                     wqs_f[:], pr_big[sl][:].rearrange("p (d h) -> p d h", h=NH), axis=AX.X),
                 inc=vd, waits=[(sp_s[sl], t_ld), (vd, vd.n)])
            t_c = P.op("vector",
                       lambda e, k=k, c=c: e.tensor_scalar(
                           wcomb[:, k * QW + c * (PQF // NH):k * QW + (c + 1) * (PQF // NH)],
                           wqs_f[:], s_att, None, ALU.mult),
                       inc=vd, waits=[(vd, vd.n)])
            prb_free[sl] = (vd, t_c)
    for mi, wsrc in enumerate([wk_d, wv_d]):
        for k in range(DK):
            sl = k % 4
            t_ld = P.dma("scalar", pr_big[sl][:, 0:DQK],
                         wsrc[k * 128:(k + 1) * 128, :], sp_s[sl], waits=[prb_free[sl]])
            t_c = P.op("vector",
                       lambda e, sl=sl, k=k, mi=mi: e.tensor_copy(
                           wcomb[:, k * QW + (1 + mi) * DQK:k * QW + (2 + mi) * DQK],
                           pr_big[sl][:, 0:DQK]),
                       inc=vd, waits=[(sp_s[sl], t_ld), (vd, vd.n)])
            prb_free[sl] = (vd, t_c)

    wcomb_done = (vd, vd.n)

    prep_jobs = []
    for mi, mat in enumerate([wg_d, wu_d, wd_d]):
        for s in range(NS):
            prep_jobs.append((mi, s))
    prep_marks = {0: [], 1: [], 2: []}
    prep_i = [0]
    qst_free = [(vd, 0)] * 2
    qsti = [0]

    def emit_prep_jobs(n):
        for _ in range(n):
            if prep_i[0] >= len(prep_jobs):
                return
            mi, s = prep_jobs[prep_i[0]]
            prep_i[0] += 1
            mat = [wg_d, wu_d, wd_d][mi]
            use_dve = (prep_i[0] % 3) == 0
            t_r = 0
            for c in range(PQ):
                sl = (prep_i[0] * PQ + c) % 4
                col = (mi * NS + s) * PQ + c
                t_ld = P.dma("sync", pr_big[sl][:],
                             mat[s][:, c * PQF:(c + 1) * PQF], sp_s[sl],
                             waits=[prb_free[sl]])
                if use_dve:
                    t_r = P.op("vector",
                               lambda e, sl=sl, col=col: e.tensor_reduce(
                                   parts[:, col:col + 1], pr_big[sl][:], axis=AX.X,
                                   op=ALU.add, apply_absolute_value=True),
                               inc=vd, waits=[(sp_s[sl], t_ld), (vd, vd.n)])
                    prb_free[sl] = (vd, t_r)
                else:
                    t_r = P.op("scalar",
                               lambda e, sl=sl, col=col: e.activation(
                                   pjunk[:], pr_big[sl][:], AF.Abs,
                                   accum_out=parts[:, col:col + 1]),
                               inc=va, waits=[(sp_s[sl], t_ld), (va, va.n)])
                    prb_free[sl] = (va, t_r)
            prep_marks[mi].append(((vd, t_r) if use_dve else (va, t_r)))

    x0_free = [(vd, 0), (vd, 0)]
    xt_free = [(vp, 0)] * 2
    r1_free = [(vd, 0), (vd, 0)]
    S_free = [(va, 0), (va, 0)]
    A_free = (vd, 0)
    A_free1 = (vd, 0)
    qkv_free = [(vd, 0), (vd, 0)]
    pstr_free = [(va, 0), (va, 0)]
    psqkv_free = [(va, 0), (va, 0)]
    x1b_w = {}
    prep_per_tile = (len(prep_jobs) + NT - 1) // NT

    xt_ld = {}
    xt_ld[0] = P.dma("gpsimd", xt_sb[0][:], x0t_d[0], xw_s[0], waits=[xt_free[0]])
    for t in range(NT):
        xs = t % 2
        t_x0 = P.dma("scalar", x0_sb[xs][:], x0_d[t * 128:(t + 1) * 128, :], xa_s[xs],
                     waits=[x0_free[xs]])
        xts = t % 2
        if t + 1 < NT:
            nts = (t + 1) % 2
            xt_ld[t + 1] = P.dma("gpsimd", xt_sb[nts][:], x0t_d[t + 1], xw_s[nts],
                                 waits=[xt_free[nts]])
        t_xt = xt_ld[t]
        qs_ = t % 2
        for k in range(DK):
            P.op("tensor",
                 lambda e, k=k, xts=xts, qs_=qs_: e.matmul(
                     ps_qkv[qs_][:], xt_sb[xts][:, k * 128:(k + 1) * 128],
                     wcomb[:, k * QW:(k + 1) * QW], start=(k == 0), stop=False),
                 waits=([(xw_s[xts], t_xt), psqkv_free[qs_], wcomb_done] if k == 0 else []))
        t_qkv = P.op("tensor",
                     lambda e, qs_=qs_: e.matmul(
                         ps_qkv[qs_][:], ones_row[:], brow[:], start=False, stop=True),
                     inc=vp)
        xt_free[xts] = (vp, t_qkv)
        t_qev = P.op("scalar",
                     lambda e, qs_=qs_: e.copy(qkvt[qs_][:], ps_qkv[qs_][:]),
                     inc=va, waits=[(vp, t_qkv), qkv_free[qs_], (va, va.n)])
        psqkv_free[qs_] = (va, t_qev)
        P.op("vector", lambda e, qs_=qs_: e.reduce_max(
            sM[:, 0:1], qkvt[qs_][:, 0:DQK], axis=AX.X, apply_absolute_value=True),
            inc=vd, waits=[(va, t_qev), (vd, vd.n)])
        P.op("vector", lambda e, qs_=qs_: e.reduce_max(
            sM[:, 1:2], qkvt[qs_][:, DQK:2 * DQK], axis=AX.X, apply_absolute_value=True),
            inc=vd, waits=[(vd, vd.n)])
        P.op("vector", lambda e: e.tensor_tensor(
            sM[:, 2:3], sM[:, 0:1], sM[:, 1:2], ALU.mult), inc=vd, waits=[(vd, vd.n)])
        t_negm = P.op("vector", lambda e: e.tensor_scalar(
            sM[:, 3:4], sM[:, 2:3], -1.0, None, ALU.mult), inc=vd, waits=[(vd, vd.n)])
        t_vb = P.op("vector", lambda e, qs_=qs_: e.tensor_copy(
            vbf[:], qkvt[qs_][:, 2 * DQK:3 * DQK]), inc=vd)
        num_last = None
        A_free2 = [A_free, A_free1]
        t_S = {}
        t_A = {}

        def emit_S(qq):
            ss = qq % 2
            t_S[qq] = P.op("vector",
                           lambda e, ss=ss, qq=qq, qs_=qs_: e.tensor_tensor(
                               S_sb[ss][:].rearrange("p (d ee) -> p d ee", d=QD),
                               qkvt[qs_][:, qq * QD:(qq + 1) * QD].unsqueeze(-1).broadcast_to([128, QD, DQK]),
                               qkvt[qs_][:, DQK:2 * DQK].unsqueeze(1).broadcast_to([128, QD, DQK]),
                               ALU.mult),
                           inc=vd, waits=[S_free[ss], (vd, vd.n)])

        def emit_exp(qq):
            emit_prep_jobs(2)
            ss = qq % 2
            Abuf = A_sb if qq % 2 == 0 else A2_sb
            t_A[qq] = P.op("scalar",
                           lambda e, ss=ss, Abuf=Abuf: e.activation(
                               Abuf[:, 0:QD * DQK], S_sb[ss][:], AF.Exp,
                               bias=sM[:, 3:4], scale=1.0),
                           inc=va, waits=[(vd, t_S[qq]), (vd, t_negm),
                                          A_free2[qq % 2], (va, va.n)])
            S_free[ss] = (va, t_A[qq])

        emit_S(0)
        emit_exp(0)
        if NQ > 1:
            emit_S(1)
            emit_exp(1)
        for qq in range(NQ):
            Abuf = A_sb if qq % 2 == 0 else A2_sb
            t_den = P.op("vector",
                         lambda e, qq=qq, Abuf=Abuf: e.reduce_sum(
                             denb[:, qq * QD:(qq + 1) * QD],
                             Abuf[:, 0:QD * DQK].rearrange("p (d ee) -> p d ee", d=QD),
                             axis=AX.X),
                         inc=vd, waits=[(va, t_A[qq]), (vd, vd.n)])
            P.op("vector",
                 lambda e, Abuf=Abuf: e.tensor_tensor(
                     Abuf[:, 0:QD * DQK].rearrange("p (d ee) -> p d ee", d=QD),
                     Abuf[:, 0:QD * DQK].rearrange("p (d ee) -> p d ee", d=QD),
                     vbf[:].unsqueeze(1).broadcast_to([128, QD, DQK]), ALU.mult),
                 inc=vd, waits=[(vd, t_den), (vd, t_vb)])
            t_num = P.op("vector",
                         lambda e, qq=qq, Abuf=Abuf: e.reduce_sum(
                             numb[:, qq * QD:(qq + 1) * QD],
                             Abuf[:, 0:QD * DQK].rearrange("p (d ee) -> p d ee", d=QD),
                             axis=AX.X),
                         inc=vd, waits=[(vd, vd.n)])
            A_free2[qq % 2] = (vd, t_num)
            num_last = t_num
            if qq + 2 < NQ:
                emit_S(qq + 2)
                emit_exp(qq + 2)
        A_free, A_free1 = A_free2[0], A_free2[1]
        qkv_free[qs_] = (vd, num_last)
        P.op("vector", lambda e: e.reciprocal(denb[:], denb[:]), inc=vd,
             waits=[(vd, vd.n)])
        t_o = P.op("vector", lambda e: e.tensor_tensor(
            numb[:], numb[:], denb[:], ALU.mult), inc=vd, waits=[(vd, vd.n)])
        rs = t % 2
        t_r1 = P.op("vector",
                    lambda e, rs=rs, xs=xs: e.tensor_tensor(
                        r1_sb[rs][:].rearrange("p (d h) -> p d h", h=NH),
                        x0_sb[xs][:].rearrange("p (d h) -> p d h", h=NH),
                        numb[:].unsqueeze(-1).broadcast_to([128, DQK, NH]), ALU.add),
                    inc=vd, waits=[(vd, t_o), r1_free[rs], (xa_s[xs], t_x0)])
        x0_free[xs] = (vd, t_r1)
        P.op("vector", lambda e, rs=rs: e.reduce_sum(
            sM[:, 0:1], r1_sb[rs][:], axis=AX.X), inc=vd, waits=[(vd, t_r1)])
        t_nm = P.op("vector", lambda e: e.tensor_scalar(
            sM[:, 1:2], sM[:, 0:1], -1.0 / D, None, ALU.mult), inc=vd,
            waits=[(vd, vd.n)])
        t_ce = P.op("scalar", lambda e, rs=rs: e.activation(
            r1_sb[rs][:], r1_sb[rs][:], AF.Identity, bias=sM[:, 1:2]), inc=va,
            waits=[(vd, t_nm), (va, va.n)])
        t_sq = P.op("scalar", lambda e, rs=rs: e.activation(
            A_sb[:, 0:D], r1_sb[rs][:], AF.Square, accum_out=sM[:, 2:3]), inc=va,
            waits=[A_free, (va, t_ce)])
        A_free = (va, t_sq)
        P.op("vector", lambda e: e.tensor_scalar(
            sM[:, 2:3], sM[:, 2:3], 1.0 / D, EPS, ALU.mult, ALU.add), inc=vd,
            waits=[(va, t_sq), (vd, vd.n)])
        t_sd = P.op("scalar", lambda e: e.activation(
            sM[:, 3:4], sM[:, 2:3], AF.Sqrt), inc=va, waits=[(vd, vd.n)])
        t_rsd = P.op("vector", lambda e: e.reciprocal(sM[:, 3:4], sM[:, 3:4]),
                     inc=vd, waits=[(va, t_sd)])
        t_x1 = P.op("scalar", lambda e, rs=rs: e.activation(
            r1_sb[rs][:], r1_sb[rs][:], AF.Copy, scale=sM[:, 3:4]), inc=va,
            waits=[(vd, t_rsd), (va, va.n)])
        t_x1b = P.dma("gpsimd", x1b_d[t * 128:(t + 1) * 128, :], r1_sb[rs][:],
                      xm_s[rs], waits=[(va, t_x1)])
        r1_free[rs] = (xm_s[rs], t_x1b)
        x1b_w[t] = t_x1b

    emit_prep_jobs(len(prep_jobs))

    fin_mark = {}
    for mi in range(3):
        pm = {}
        for (s_, v_) in prep_marks[mi]:
            pm[s_.name] = (s_, max(v_, pm.get(s_.name, (s_, 0))[1]))
        t_f = P.op("vector",
                   lambda e, mi=mi: e.reduce_sum(
                       parts[:, 3 * NS * PQ + mi:3 * NS * PQ + mi + 1],
                       parts[:, mi * NS * PQ:(mi + 1) * NS * PQ], axis=AX.X),
                   inc=vd, waits=list(pm.values()) + [(vd, vd.n)])
        t_mm = P.op("tensor",
                    lambda e, mi=mi: e.matmul(
                        ps_sc[0:1, mi:mi + 1],
                        parts[:, 3 * NS * PQ + mi:3 * NS * PQ + mi + 1],
                        onesf_col[:], start=True, stop=True),
                    inc=vp, waits=[(vd, t_f)])
        fin_mark[mi] = t_mm
    t_cp = P.op("scalar", lambda e: e.copy(srow[0:1, 0:3], ps_sc[0:1, 0:3]),
                inc=va, waits=[(vp, fin_mark[2]), (va, va.n)])
    P.op("vector", lambda e: e.tensor_scalar(
        srow[0:1, 4:7], srow[0:1, 0:3], 1.0 / NW, None, ALU.mult), inc=vd,
        waits=[(va, t_cp), (vd, vd.n)])
    P.op("vector", lambda e: e.tensor_scalar(
        srow[0:1, 8:11], srow[0:1, 4:7], EPS, None, ALU.add), inc=vd,
        waits=[(vd, vd.n)])
    P.op("vector", lambda e: e.reciprocal(srow[0:1, 8:11], srow[0:1, 8:11]),
         inc=vd, waits=[(vd, vd.n)])
    P.op("vector", lambda e: e.tensor_tensor(
        srow[0:1, 12:13], srow[0:1, 4:5], srow[0:1, 5:6], ALU.mult), inc=vd,
        waits=[(vd, vd.n)])
    P.op("vector", lambda e: e.tensor_tensor(
        srow[0:1, 11:12], srow[0:1, 12:13], srow[0:1, 6:7], ALU.mult), inc=vd,
        waits=[(vd, vd.n)])
    t_bc = P.op("tensor", lambda e: e.matmul(
        ps_bc[:], onesf_row[:], srow[0:1, 8:12], start=True, stop=True),
        inc=vp, waits=[(vd, vd.n)])
    t_rm = P.op("scalar", lambda e: e.copy(rmbc[:], ps_bc[:]), inc=va,
                waits=[(vp, t_bc), (va, va.n)])
    scales_ready = (va, t_rm)
    barrier_snap = ([(vd, vd.n), (va, va.n), (vp, vp.n), (vg, vg.n)] +
                    [(s, s.n) for s in sq_s + sp_s + qw_s + xa_s + xw_s + xm_s] +
                    [(s_mi, s_mi.n)])

    esA.close()
    for eng in ("sync", "scalar", "gpsimd", "vector", "tensor"):
        for (s, v) in barrier_snap:
            P.wait(eng, s, v)

    esF = contextlib.ExitStack()
    x1tg = esF.enter_context(nc.sbuf_tensor("x1tg", [128, DK * PT], BF16))
    qring = [esF.enter_context(nc.sbuf_tensor(f"qr{i}", [128, D], BF16)) for i in range(3)]
    inter = esF.enter_context(nc.sbuf_tensor("inter", [128, NS * PT], BF16))
    sbg = esF.enter_context(nc.sbuf_tensor("sbg", [128, PT], BF16))
    sm2 = esF.enter_context(nc.sbuf_tensor("sm2", [128, 4], F32))
    esQ = contextlib.ExitStack()
    qraw = [esQ.enter_context(nc.sbuf_tensor(f"qraw{i}", [128, QF], F32)) for i in range(4)]
    qstage = [esQ.enter_context(nc.sbuf_tensor(f"qst{i}", [128, QF], BF16)) for i in range(3)]

    qw_marks = {}

    qci = [0]
    qci2 = [0]

    def quant_block(mat_i, s, eng, esem, _slq):
        src_d = [wg_d, wu_d, wd_d][mat_i]
        dst_d = qc_d[mat_i]
        rm_ap = rmbc[:, mat_i:mat_i + 1]
        wrs = []
        for c in range(QCH):
            if eng == "vector":
                slq = qci[0] % 2
                qci[0] += 1
            else:
                slq = 2 + qci2[0] % 2
                qci2[0] += 1
            t_ld = P.dma("sync", qraw[slq][:],
                         src_d[s][:, c * QF:(c + 1) * QF], sq_s[slq],
                         waits=[qraw_free[slq]])
            st = qsti[0] % 2
            qsti[0] += 1
            if eng == "vector":
                P.op(eng, lambda e, slq=slq: e.tensor_scalar(
                    qraw[slq][:], qraw[slq][:], rm_ap, -1.0, ALU.mult, ALU.max),
                    inc=esem, waits=[(sq_s[slq], t_ld), scales_ready, (esem, esem.n)])
                P.op(eng, lambda e, slq=slq: e.tensor_scalar(
                    qraw[slq][:], qraw[slq][:], 1.0, C_RND, ALU.min, ALU.add),
                    inc=esem, waits=[(esem, esem.n)])
                t_q = P.op(eng, lambda e, slq=slq, st=st: e.tensor_scalar(
                    qstage[st][:], qraw[slq][:], -C_RND, None, ALU.add),
                    inc=esem, waits=[(esem, esem.n), qst_free[st]])
            else:
                P.op(eng, lambda e, slq=slq: e.activation(
                    qraw[slq][:], qraw[slq][:], AF.Identity, scale=rm_ap),
                    inc=esem, waits=[(sq_s[slq], t_ld), scales_ready, (esem, esem.n)])
                P.op(eng, lambda e, slq=slq: e.activation(
                    qraw[slq][:], qraw[slq][:], AF.Relu, scale=-1.0, bias=cbias[:, 0:1]),
                    inc=esem, waits=[(esem, esem.n)])
                P.op(eng, lambda e, slq=slq: e.activation(
                    qraw[slq][:], qraw[slq][:], AF.Relu, scale=-1.0, bias=cbias[:, 1:2]),
                    inc=esem, waits=[(esem, esem.n)])
                P.op(eng, lambda e, slq=slq: e.activation(
                    qraw[slq][:], qraw[slq][:], AF.Identity, bias=cbias[:, 2:3]),
                    inc=esem, waits=[(esem, esem.n)])
                t_q = P.op(eng, lambda e, slq=slq, st=st: e.activation(
                    qstage[st][:], qraw[slq][:], AF.Identity, bias=cbias[:, 3:4]),
                    inc=esem, waits=[(esem, esem.n), qst_free[st]])
            qraw_free[slq] = (esem, t_q)
            t_wr = P.dma("gpsimd", dst_d[s][:, c * QF:(c + 1) * QF], qstage[st][:],
                         qw_s[st], waits=[(esem, t_q)])
            qst_free[st] = (qw_s[st], t_wr)
            wrs.append((qw_s[st], t_wr))
        qw_marks[(mat_i, s)] = wrs

    QLEAD = 3
    for s in range(min(QLEAD, NS)):
        quant_block(0, s, "vector", vd, None)
        quant_block(1, s, "vector", vd, None)
    quant_next = [QLEAD]
    dquant_next = [0]

    def emit_gu_quant():
        s = quant_next[0]
        if s < NS:
            quant_block(0, s, "vector", vd, None)
            quant_block(1, s, "vector", vd, None)
            quant_next[0] += 1

    def emit_d_quant(n=1):
        for _ in range(n):
            s2 = dquant_next[0]
            if s2 < NS:
                quant_block(2, s2, "scalar", va, None)
                dquant_next[0] += 1


    qr_free = [(vp, 0)] * 3
    x1tg_free = (vp, 0)
    inter_rd_done = (vp, 0)
    t1_free = [(va, 0)] * TSP
    r2_free = (vd, 0)
    t1 = None
    r2 = None
    down_ev_last = (vd, 0)

    for p in range(NP):
        t_x1tg = 0
        for k in range(DK):
            t_x1tg = P.op(
                "scalar",
                lambda e, k=k, p=p: e.dma_start(
                    x1tg[:, k * PT:(k + 1) * PT],
                    x1b_d[p * PT:(p + 1) * PT, k * 128:(k + 1) * 128],
                    transpose=True),
                inc=s_tg, inck=16,
                waits=([x1tg_free] if k == 0 else []))
        PSB = max(PT, 512)
        NPAR = 4
        with nc.psum_tensor(f"ps_g{p}", [128, NPAR * PSB], F32) as ps_g, \
             nc.psum_tensor(f"ps_u{p}", [128, NPAR * PSB], F32) as ps_u:
            evac_by_par = [down_ev_last] * NPAR
            iv_marks = {}
            par = 0
            for s in range(NS):
                if p == 0:
                    emit_gu_quant()
                    emit_d_quant(1)
                slg = (2 * s) % 3
                slu = (2 * s + 1) % 3
                t_lg = P.dma("sync", qring[slg][:], qc_d[0][s], qc_s[slg],
                             waits=[qr_free[slg]] + qw_marks[(0, s)])
                t_lu = P.dma("sync", qring[slu][:], qc_d[1][s], qc_s[slu],
                             waits=[qr_free[slu]] + qw_marks[(1, s)])
                qlg = (qc_s[slg], t_lg)
                qlu = (qc_s[slu], t_lu)
                pg = ps_g[:, par * PSB:par * PSB + PT]
                pu = ps_u[:, par * PSB:par * PSB + PT]
                for k in range(DK):
                    P.op("tensor",
                         lambda e, k=k, slg=slg, pg=pg: e.matmul(
                             pg, qring[slg][:, k * 128:(k + 1) * 128],
                             x1tg[:, k * PT:(k + 1) * PT],
                             start=(k == 0), stop=(k == DK - 1)),
                         inc=(vp if k == DK - 1 else None),
                         waits=([qlg, (s_tg, t_x1tg), evac_by_par[par]]
                                if k == 0 else []))
                t_g = vp.n
                for k in range(DK):
                    P.op("tensor",
                         lambda e, k=k, slu=slu, pu=pu: e.matmul(
                             pu, qring[slu][:, k * 128:(k + 1) * 128],
                             x1tg[:, k * PT:(k + 1) * PT],
                             start=(k == 0), stop=(k == DK - 1)),
                         inc=(vp if k == DK - 1 else None),
                         waits=([qlu] if k == 0 else []))
                t_u = vp.n
                qr_free[slg] = (vp, t_g)
                qr_free[slu] = (vp, t_u)
                sbgs = sbg[:, 0:PT]
                t_cp = P.op("vector", lambda e, pg=pg, sbgs=sbgs: e.tensor_copy(sbgs, pg),
                            inc=vd, waits=[(vp, t_g), (vd, vd.n), evac_by_par[par]])
                t_iv = P.op("vector",
                            lambda e, s=s, pu=pu, sbgs=sbgs: e.tensor_tensor(
                                inter[:, s * PT:(s + 1) * PT], sbgs, pu, ALU.mult),
                            inc=vd, waits=[(vd, t_cp), (vp, t_u),
                                           inter_rd_done])
                iv_marks[s] = t_iv
                evac_by_par[par] = (vd, t_iv)
                par = (par + 1) % NPAR
            if p == 0:
                while quant_next[0] < NS:
                    emit_gu_quant()
                emit_d_quant(NS)
            last_gu_evac = (vd, iv_marks[NS - 1])
        if p == 0:
            esQ.close()
            t1 = [esF.enter_context(nc.sbuf_tensor(f"t1_{i}", [128, D], BF16)) for i in range(TSP)]
            r2 = esF.enter_context(nc.sbuf_tensor("r2", [128, D], F32))
            qdone = ([(vd, vd.n), (va, va.n)] + [(q, q.n) for q in qw_s] +
                     [(q, q.n) for q in sq_s])
            t1_free = [qdone] * TSP
            r2_free = qdone
        with nc.psum_tensor(f"ps_d{p}", [128, TSP * DQCH], F32) as ps_d:
            ev_all = down_ev_last
            for dq in range(NDQ):
                dq_ev_prev = ev_all
                for s in range(NS):
                    sl = s % 3
                    t_ld = P.dma("sync",
                                 qring[sl][:, 0:DQCH],
                                 qc_d[2][s][:, dq * DQCH:(dq + 1) * DQCH],
                                 qc_s[sl], waits=[qr_free[sl]] + qw_marks[(2, s)])
                    first = True
                    for ts in range(TSP):
                        for ch in range(NCH):
                            w = []
                            if first:
                                w = [(qc_s[sl], t_ld), last_gu_evac, dq_ev_prev,
                                     (vd, iv_marks[s])]
                                first = False
                            P.op("tensor",
                                 lambda e, sl=sl, s=s, ts=ts, ch=ch: e.matmul(
                                     ps_d[:, ts * DQCH + ch * MMN:ts * DQCH + (ch + 1) * MMN],
                                     inter[:, s * PT + ts * 128:s * PT + (ts + 1) * 128],
                                     qring[sl][:, ch * MMN:(ch + 1) * MMN],
                                     start=(s == 0), stop=(s == NS - 1)),
                                 inc=(vp if (ts == TSP - 1 and ch == NCH - 1) else None),
                                 waits=w)
                    qr_free[sl] = (vp, vp.n)
                t_dq = vp.n
                for ts in range(TSP):
                    t_ev = P.op("vector",
                                lambda e, ts=ts, dq=dq: e.tensor_scalar(
                                    t1[ts][:, dq * DQCH:(dq + 1) * DQCH],
                                    ps_d[:, ts * DQCH:(ts + 1) * DQCH],
                                    rmbc[:, 3:4], None, ALU.mult),
                                inc=vd,
                                waits=(t1_free[ts] if isinstance(t1_free[ts], list)
                                       else [t1_free[ts]]) + [(vp, t_dq), (vd, vd.n)])
                    ev_all = (vd, t_ev)
            down_ev_last = ev_all
            for ts in range(TSP):
                trow = p * TSP + ts
                t_xl = P.dma("gpsimd", r2[:], x1b_d[trow * 128:(trow + 1) * 128, :],
                             s_r2, waits=(r2_free if isinstance(r2_free, list)
                                          else [r2_free]))
                t_r2 = P.op("vector",
                            lambda e, ts=ts: e.tensor_tensor(
                                r2[:], r2[:], t1[ts][:], ALU.add),
                            inc=vd, waits=[(s_r2, t_xl), down_ev_last, (vd, vd.n)])
                P.op("vector", lambda e: e.reduce_sum(
                    sm2[:, 0:1], r2[:], axis=AX.X), inc=vd, waits=[(vd, t_r2)])
                t_nm = P.op("vector", lambda e: e.tensor_scalar(
                    sm2[:, 1:2], sm2[:, 0:1], -1.0 / D, None, ALU.mult), inc=vd,
                    waits=[(vd, vd.n)])
                t_ce = P.op("scalar", lambda e: e.activation(
                    r2[:], r2[:], AF.Identity, bias=sm2[:, 1:2]), inc=va,
                    waits=[(vd, t_nm), (va, va.n)])
                t_sq2 = P.op("scalar", lambda e, ts=ts: e.activation(
                    t1[ts][:], r2[:], AF.Square, accum_out=sm2[:, 2:3]), inc=va,
                    waits=[(va, t_ce)])
                t1_free[ts] = (va, t_sq2)
                P.op("vector", lambda e: e.tensor_scalar(
                    sm2[:, 2:3], sm2[:, 2:3], 1.0 / D, EPS, ALU.mult, ALU.add),
                    inc=vd, waits=[(va, t_sq2), (vd, vd.n)])
                t_sd = P.op("scalar", lambda e: e.activation(
                    sm2[:, 3:4], sm2[:, 2:3], AF.Sqrt), inc=va, waits=[(vd, vd.n)])
                P.op("vector", lambda e: e.reciprocal(sm2[:, 3:4], sm2[:, 3:4]),
                     inc=vd, waits=[(va, t_sd)])
                t_o2 = P.op("scalar", lambda e: e.activation(
                    r2[:], r2[:], AF.Copy, scale=sm2[:, 3:4]), inc=va,
                    waits=[(vd, vd.n), (va, va.n)])
                t_wr = P.dma("gpsimd", out_d[trow * 128:(trow + 1) * 128, :],
                             r2[:], s_out, waits=[(va, t_o2)])
                r2_free = (s_out, t_wr)
        x1tg_free = (vp, vp.n)
        inter_rd_done = (vp, vp.n)
    esF.close()

    for s in sq_s + sp_s + qc_s:
        P.wait("sync", s, s.n)
    for s in [s_tg, s_mi] + xa_s:
        P.wait("scalar", s, s.n)
    for s in qw_s + xw_s + xm_s + [s_out, s_r2]:
        P.wait("gpsimd", s, s.n)

    with nc.Block() as block:
        @block.sync
        def _(e):
            for f in P.q["sync"]:
                f(e)

        @block.scalar
        def _(e):
            for f in P.q["scalar"]:
                f(e)

        @block.vector
        def _(e):
            for f in P.q["vector"]:
                f(e)

        @block.tensor
        def _(e):
            for f in P.q["tensor"]:
                f(e)

        @block.gpsimd
        def _(e):
            for f in P.q["gpsimd"]:
                f(e)

    es.close()
    return nc


def stage_inputs(L0_attn_x, L0_attn_Wq, L0_attn_bq, L0_attn_Wk, L0_attn_bk,
                 L0_attn_Wv, L0_attn_bv, L0_ffn_w_gate, L0_ffn_w_up, L0_ffn_w_down):
    x = np.asarray(L0_attn_x)
    B, T, D = x.shape
    I = np.asarray(L0_ffn_w_gate).shape[0]
    NS = I // 128
    DK = D // 128
    wq = np.ascontiguousarray(np.asarray(L0_attn_Wq).reshape(D, -1), dtype=np.float32)
    bq = np.ascontiguousarray(np.asarray(L0_attn_bq).reshape(1, -1), dtype=np.float32)
    wk = np.ascontiguousarray(np.asarray(L0_attn_Wk)[:, :, 0], dtype=np.float32)
    bk = np.ascontiguousarray(np.asarray(L0_attn_bk).reshape(1, -1), dtype=np.float32)
    wv = np.ascontiguousarray(np.asarray(L0_attn_Wv)[:, :, 0], dtype=np.float32)
    bv = np.ascontiguousarray(np.asarray(L0_attn_bv).reshape(1, -1), dtype=np.float32)
    wgT = np.asarray(L0_ffn_w_gate, dtype=np.float32).T
    wuT = np.asarray(L0_ffn_w_up, dtype=np.float32).T
    wg = np.ascontiguousarray(
        wgT.reshape(DK, 128, NS, 128).transpose(2, 1, 0, 3).reshape(NS, 128, D))
    wu = np.ascontiguousarray(
        wuT.reshape(DK, 128, NS, 128).transpose(2, 1, 0, 3).reshape(NS, 128, D))
    wd = np.ascontiguousarray(
        np.asarray(L0_ffn_w_down, dtype=np.float32).T.reshape(NS, 128, D))
    ident = np.eye(128, dtype=np.float32)
    common = {"wq": wq, "bq": bq, "wk": wk, "bk": bk, "wv": wv, "bv": bv,
              "wg": wg, "wu": wu, "wd": wd, "ident": ident}
    in_maps = []
    for c in range(B):
        m = dict(common)
        m["x0"] = np.ascontiguousarray(x[c], dtype=np.float32)
        xc = np.asarray(x[c], dtype=np.float32)
        m["x0t"] = np.ascontiguousarray(
            xc.reshape(T // 128, 128, D // 128, 128).transpose(0, 3, 2, 1).reshape(T // 128, 128, D))
        in_maps.append(m)
    return in_maps


def kernel(**inputs):
    in_maps = stage_inputs(**inputs)
    B = len(in_maps)
    T, D = in_maps[0]["x0"].shape
    I = in_maps[0]["wg"].shape[0] * 128
    nc = build_program(T=T, D=D, I=I)
    res = run_bass_kernel_spmd(nc, in_maps, core_ids=list(range(N_CORES)))
    out = np.stack([np.asarray(res.results[c]["out"]) for c in range(B)], axis=0)
    return out.astype(np.float32)
